# version 04
# baseline (speedup 1.0000x reference)
import numpy as np
import ml_dtypes
from contextlib import ExitStack

import concourse.bass as bass
import concourse.mybir as mybir
from concourse.bass_utils import run_bass_kernel_spmd

F32 = mybir.dt.float32
BF16 = mybir.dt.bfloat16
FP8 = mybir.dt.float8e4
U8 = mybir.dt.uint8
AF = mybir.ActivationFunctionType
ALU = mybir.AluOpType
AX = mybir.AxisListType
DRMODE = mybir.MatmulPerfMode.DoubleRow

NCORE = 8
RPC = 2048
C = 4096
NRT = 16
NRC = 4
NTT = 32
NOT = 32
NIT = 32
NPAIR = 16
W16 = 16


class Marks:
    def __init__(self, dry, marks):
        self.dry = dry
        self.marks = marks
        self.cnt = {}

    def bump(self, sem, n):
        k = id(sem)
        self.cnt[k] = self.cnt.get(k, 0) + n
        return self.cnt[k]

    def mark(self, label, sem):
        if self.dry:
            self.marks[label] = self.cnt.get(id(sem), 0)

    def val(self, label):
        if self.dry:
            return 0
        if label in self.marks:
            return self.marks[label]
        if "-" in label:
            return 0
        raise KeyError(f"unknown mark {label}")


def build_nc(marks, dry, nc):
    E = Marks(dry, marks)

    t_x = nc.dram_tensor("x", [RPC, C], F32, kind="ExternalInput")
    t_w = nc.dram_tensor("w", [C, C], F32, kind="ExternalInput")
    t_wsh = nc.dram_tensor("wsh", [512, C], F32, kind="ExternalInput")
    t_bias = nc.dram_tensor("bias", [C, 1], F32, kind="ExternalInput")
    t_lam = nc.dram_tensor("lam", [64, 2], F32, kind="ExternalInput")
    t_pp = nc.dram_tensor("pp", [64, 2], F32, kind="ExternalInput")
    t_qq = nc.dram_tensor("qq", [64, 2], F32, kind="ExternalInput")
    t_ct = nc.dram_tensor("ct", [64, 2], F32, kind="ExternalInput")
    t_bp = nc.dram_tensor("bp", [64, 1], F32, kind="ExternalInput")
    t_dd = nc.dram_tensor("dd", [1, 1], F32, kind="ExternalInput")
    t_ls = nc.dram_tensor("ls", [1, 1], F32, kind="ExternalInput")
    t_vctre = nc.dram_tensor("vctre", [64, 64], F32, kind="ExternalInput")
    t_vctim = nc.dram_tensor("vctim", [64, 64], F32, kind="ExternalInput")
    t_tb = nc.dram_tensor("tb", [64, C], F32, kind="ExternalInput")
    t_tg = nc.dram_tensor("tg", [64, 64], F32, kind="ExternalInput")
    t_e1re = nc.dram_tensor("e1re", [64, 64], F32, kind="ExternalInput")
    t_e1im = nc.dram_tensor("e1im", [64, 64], F32, kind="ExternalInput")
    t_w2re = nc.dram_tensor("w2re", [64, 64], F32, kind="ExternalInput")
    t_w2im = nc.dram_tensor("w2im", [64, 64], F32, kind="ExternalInput")
    t_e2re = nc.dram_tensor("e2re", [64, 64], F32, kind="ExternalInput")
    t_e2imn = nc.dram_tensor("e2imn", [64, 64], F32, kind="ExternalInput")
    t_idb = nc.dram_tensor("idb", [128, 128], BF16, kind="ExternalInput")
    t_id8 = nc.dram_tensor("id8", [128, 128], FP8, kind="ExternalInput")
    t_onesf = nc.dram_tensor("onesf", [1, 128], F32, kind="ExternalInput")

    t_out = nc.dram_tensor("out", [RPC, C], F32, kind="ExternalOutput")

    t_kpad = nc.dram_tensor("kpad_d", [1, 8832], BF16)
    t_st = nc.dram_tensor("st_d", [C, RPC], BF16)
    t_wqt = nc.dram_tensor("wqt_d", [C, C], FP8)
    t_ccin = nc.dram_tensor("cc_in", [1, 2], F32)
    t_ccout = nc.dram_tensor("cc_out", [1, 2], F32)

    xap = t_x.ap()
    stap = t_st.ap()
    wqtap = t_wqt.ap()

    es = ExitStack()
    with es:
        sb = lambda name, shape, dt: es.enter_context(nc.sbuf_tensor(name, shape, dt))
        arena = sb("arena", [128, 65536], U8)
        xnt = [arena[:, 0:32768].bitcast(BF16), arena[:, 32768:65536].bitcast(BF16)]

        kstack = sb("kstack", [128, C], BF16)
        xst = [sb(f"xst{i}", [128, C], F32) for i in range(2)]
        xnb = sb("xnb", [128, C], BF16)
        wld = [sb(f"wld{i}", [128, 2048], F32) for i in range(2)]
        wqb = sb("wqb", [128, C], FP8)
        wqt = sb("wqt", [128, C], FP8)
        wqldb = [sb(f"wqld{i}", [128, C], FP8) for i in range(2)]
        stld = [sb(f"stld{i}", [128, 512], BF16) for i in range(2)]
        gst = [sb(f"gst{i}", [128, 512], BF16) for i in range(2)]
        sgst = [sb(f"sgst{i}", [128, 512], BF16) for i in range(2)]
        xcb = [sb(f"xcb{i}", [128, 512], F32) for i in range(2)]
        outp = [sb(f"outp{i}", [128, 512], F32) for i in range(2)]
        stv = [sb(f"stv{i}", [128, 512], BF16) for i in range(4)]
        q8r = [sb(f"q8r{i}", [128, C], FP8) for i in range(2)]
        rs2 = sb("rs2", [128, 16], F32)
        junkb = q8r[0][:, :]
        kkre = q8r[1][:, :].bitcast(F32)[0:4, 0:1024]
        kkim = wqb[:, :].bitcast(F32)[0:4, 0:1024]
        stat = sb("stat", [128, 16], F32)
        statm = sb("statm", [128, 16], F32)
        statr = sb("statr", [128, 16], F32)
        runm = sb("runm", [128, 1], F32)
        wamax = sb("wamax", [128, 1], F32)
        dcol = sb("dcol", [128, 1], F32)
        hcol = sb("hcol", [128, 1], F32)
        i2sxb = sb("i2sxb", [128, 1], F32)
        i2swb = sb("i2swb", [128, 1], F32)
        s4b = sb("s4b", [128, 1], F32)
        bcol = sb("bcol", [128, 32], F32)
        scl = sb("scl", [1, 16], F32)
        ccsb = sb("ccsb", [1, 2], F32)
        amax2 = sb("amax2", [1, 2], F32)
        flat2 = sb("flat2", [1, 256], F32)
        flatx = flat2[0:1, 0:128]
        flatw = flat2[0:1, 128:256]
        p_lam = sb("p_lam", [64, 2], F32)
        p_pp = sb("p_pp", [64, 2], F32)
        p_qq = sb("p_qq", [64, 2], F32)
        p_ct = sb("p_ct", [64, 2], F32)
        p_bp = sb("p_bp", [64, 1], F32)
        p_dd = sb("p_dd", [1, 1], F32)
        p_ls = sb("p_ls", [1, 1], F32)
        p_vre = sb("p_vre", [64, 64], F32)
        p_vim = sb("p_vim", [64, 64], F32)
        p_tg = sb("p_tg", [64, 64], F32)
        p_e1re = sb("p_e1re", [64, 64], F32)
        p_e1im = sb("p_e1im", [64, 64], F32)
        p_w2re = sb("p_w2re", [64, 64], F32)
        p_w2im = sb("p_w2im", [64, 64], F32)
        p_e2re = sb("p_e2re", [64, 64], F32)
        p_e2imn = sb("p_e2imn", [64, 64], F32)
        p_idb = sb("p_idb", [128, 128], BF16)
        p_id8 = sb("p_id8", [128, 128], FP8)
        p_ones = sb("p_ones", [1, 128], F32)
        tbch = [sb("tbch", [64, 1024], F32)] * 2
        dimF = sb("dimF", [64, 1024], F32)
        magF = sb("magF", [64, 1024], F32)
        freF = sb("freF", [64, 1024], F32)
        fimF = sb("fimF", [64, 1024], F32)

        krre = [sb(f"krre{q}", [64, 64], F32) for q in range(4)]
        krim = [sb(f"krim{q}", [64, 64], F32) for q in range(4)]
        vre = sb("vre", [64, 4], F32)
        vim = sb("vim", [64, 4], F32)
        vmim = sb("vmim", [64, 4], F32)
        bvec = sb("bvec", [64, 2], F32)
        kv = sb("kv", [64, 8], F32)
        ww = sb("ww", [64, 2], F32)
        arre = sb("arre", [64, 64], F32)
        arim = sb("arim", [64, 64], F32)
        narim = sb("narim", [64, 64], F32)
        tmpa = sb("tmpa", [64, 64], F32)
        tmpb = sb("tmpb", [64, 64], F32)
        As = sb("As", [64, 128], F32)
        Ats = sb("Ats", [64, 128], F32)
        kt_sb = sb("kt_sb", [64, 64], F32)
        ktb = sb("ktb", [64, 64], BF16)
        zrow = sb("zrow", [64, 16], BF16)

        pf = [es.enter_context(nc.psum_tensor(f"pf{i}", [128, 512], F32))
              for i in range(4)]
        pt = es.enter_context(nc.psum_tensor("pt", [128, 2048], F32))
        ptb = pt[:, :].bitcast(BF16)

        def ptbank(k):
            k = k % 4
            return ptb[:, 1024 * k:1024 * k + 128]

        ptb8 = pt[:, :].bitcast(FP8)

        def pt8bank(k):
            k = k % 4
            return ptb8[:, 2048 * k:2048 * k + 256:2]

        tridx = {}
        ring = [("s_sc", f"tre_{g}") for g in range(512)]
        _g = [512]

        def _reg(lab, eng):
            tridx[lab] = _g[0]
            ring.append((eng, lab))
            _g[0] += 1

        for _rt in range(NRT):
            for _pair in (2 * _rt, 2 * _rt + 1):
                for _k in range(32):
                    _reg(f"wtr_{_pair}_{_k}",
                         "s_v" if _k % 2 else "s_sc")
            for _k in range(32):
                _reg(f"qtr_{_rt}_{_k}", "s_v" if _k % 2 else "s_sc")
        E_tridx = tridx

        def evac_of(lab):
            if lab.startswith("tre_"):
                return lab
            kind, a, b = lab.split("_")
            if kind == "wtr":
                return (f"wtreV_{a}_{b}" if int(b) % 2 else f"wtreS_{a}_{b}")
            return (f"qe_{a}_{b}" if int(b) % 2 else f"qeS_{a}_{b}")

        sem = lambda n: es.enter_context(nc.semaphore(n))
        d_prm = sem("d_prm"); d_tb = sem("d_tb"); d_ks = sem("d_ks")
        d_bc = sem("d_bc")
        d_g = sem("d_g"); d_qw = sem("d_qw")
        d_x = [sem(f"d_x{i}") for i in range(2)]
        d_wsh = [sem(f"d_wsh{i}") for i in range(2)]
        d_w = [sem(f"d_w{i}") for i in range(2)]
        d_wq = [sem(f"d_wq{i}") for i in range(2)]
        d_sb = [sem(f"d_sb{i}") for i in range(2)]
        d_xc = [sem(f"d_xc{i}") for i in range(2)]
        d_out = [sem(f"d_out{i}") for i in range(2)]
        d_sw = [sem(f"d_sw{i}") for i in range(4)]
        s_sc = sem("s_sc"); s_v = sem("s_v"); s_t = sem("s_t"); s_cc = sem("s_cc")
        s_gp = sem("s_gp")

        block = es.enter_context(nc.Block())

        def emit_sync(g):
            def dma(out, in_, dsem, waits=()):
                if not dry:
                    for sm, vv in waits:
                        if vv > 0:
                            g.wait_ge(sm, vv)
                    g.dma_start(out=out, in_=in_).then_inc(dsem, W16)
                E.bump(dsem, W16)

            prm = [(p_lam, t_lam), (p_pp, t_pp), (p_qq, t_qq), (p_ct, t_ct),
                   (p_bp, t_bp), (p_dd, t_dd), (p_ls, t_ls), (p_vre, t_vctre),
                   (p_vim, t_vctim), (p_tg, t_tg), (p_e1re, t_e1re),
                   (p_e1im, t_e1im), (p_w2re, t_w2re), (p_w2im, t_w2im),
                   (p_e2re, t_e2re), (p_e2imn, t_e2imn), (p_idb, t_idb),
                   (p_id8, t_id8), (p_ones, t_onesf)]
            for sbt, drt in prm:
                dma(sbt[:, :], drt.ap(), d_prm)
            for i in range(8):
                w_ = [(s_v, E.val(f"wshmax_{i - 2}"))] if i >= 2 else ()
                dma(wld[i % 2][:, :],
                    t_wsh.ap()[(i // 2) * 128:(i // 2) * 128 + 128,
                               (i % 2) * 2048:(i % 2) * 2048 + 2048],
                    d_wsh[i % 2], waits=w_)
            for c in range(4):
                w_ = [(s_v, E.val(f"F_{c - 1}"))] if c >= 1 else ()
                dma(tbch[c % 2][:, :], t_tb.ap()[:, c * 1024:(c + 1) * 1024],
                    d_tb, waits=w_)
            for rt in range(NRT):
                w_ = []
                if rt >= 2:
                    w_ = [(s_v, E.val(f"xn_{rt - 2}")),
                          (s_sc, E.val(f"acc_{rt - 2}"))]
                dma(xst[rt % 2][:, :], xap[rt * 128:(rt + 1) * 128, :],
                    d_x[rt % 2], waits=w_)
            for ot in range(NOT):
                for h in range(2):
                    i = 2 * ot + h
                    w_ = ([(s_v, E.val(f"wsc_{i - 2}"))] if i >= 2
                          else [(s_v, E.val(f"wshmax_{6 + i}"))])
                    dma(wld[i % 2][:, :],
                        t_w.ap()[ot * 128:(ot + 1) * 128,
                                 h * 2048:h * 2048 + 2048],
                        d_w[i % 2], waits=w_)
                if ot % 2 == 0:
                    rt = ot // 2
                    w_ = [(s_v, E.val(f"xn_{14 + (rt % 2)}"))] if rt < 2 else                          [(s_v, E.val(f"q8r_{rt - 2}"))]
                    dma(xst[rt % 2][:, :], xap[rt * 128:(rt + 1) * 128, :],
                        d_x[rt % 2], waits=w_)
            def wqld_load(ot):
                w_ = [(d_qw, W16 * (ot + 1))]
                if ot >= 2:
                    w_.append((s_t, E.val(f"ffn_{ot - 2}_3")))
                dma(wqldb[ot % 2][:, :], wqtap[ot * 128:(ot + 1) * 128, :],
                    d_wq[ot % 2], waits=w_)

            wqld_load(0)
            wqld_load(1)
            for ot in range(NOT):
                if ot + 2 < NOT:
                    wqld_load(ot + 2)
                for rc in range(NRC):
                    j = 4 * ot + rc
                    w_ = ([(d_sw[s_], W16 * 32) for s_ in range(4)]
                          if j == 0 else [])
                    if j >= 2:
                        w_.append((s_v, E.val(f"sgm_{j - 2}")))
                    dma(stld[j % 2][:, :],
                        stap[ot * 128:(ot + 1) * 128, rc * 512:(rc + 1) * 512],
                        d_sb[j % 2], waits=w_)
                    w_ = []
                    if j >= 2:
                        w_.append((s_v, E.val(f"add_{j - 2}")))
                    dma(xcb[j % 2][:, :],
                        bass.AP(t_x, rc * 512 * C + ot * 128,
                                [[C, 128], [128 * C, 4], [1, 128]]),
                        d_xc[j % 2], waits=w_)

        def emit_gpsimd(g):
            def dma(out, in_, dsem, waits=()):
                if not dry:
                    for sm, vv in waits:
                        if vv > 0:
                            g.wait_ge(sm, vv)
                    g.dma_start(out=out, in_=in_).then_inc(dsem, W16)
                E.bump(dsem, W16)

            if not dry:
                g.memset(zrow[:, :], 0.0).then_inc(s_gp, 1)
            E.bump(s_gp, 1)
            for k in range(4):
                dma(t_kpad.ap()[0:1, k * 1024:(k + 1) * 1024], zrow[:, :], d_g,
                    waits=[(s_gp, 1)] if k == 0 else ())
            dma(t_kpad.ap()[0:1, 4096:4224], zrow[0:8, :], d_g)
            dma(t_kpad.ap()[0:1, 8320:8832], zrow[0:32, :], d_g)
            for c in range(4):
                for q in range(4):
                    for im_ in range(2):
                        src_ = (kkim if im_ else kkre)[q:q + 1, :]
                        dst = (krim if im_ else krre)[q][16 * c:16 * (c + 1), :]
                        dma(dst, src_, d_g,
                            waits=([(s_v, E.val(f"kkevac_{c}"))]
                                   if (q == 0 and im_ == 0) else ()))
                E.mark(f"krr_{c}", d_g)
            E.mark("kr_done", d_g)
            dma(ktb[:, :], kt_sb[:, :], d_g, waits=[(s_v, E.val("ktevac"))])
            E.mark("ktb_w", d_g)
            dma(t_kpad.ap()[0:1, 4224:8320].rearrange("a (m1 m0) -> a m1 m0",
                                                      m0=64),
                ktb[:, :], d_g, waits=[(d_g, E.val("ktb_w"))])
            E.mark("kpad_done", d_g)
            for rc in range(NRC):
                for tt in range(NTT):
                    i = 32 * rc + tt
                    dma(stap[tt * 128:(tt + 1) * 128, rc * 512:(rc + 1) * 512],
                        stv[i % 4][:, :], d_sw[i % 4],
                        waits=[(s_v, E.val(f"sevac_{rc}_{tt}"))])
            dma(flatx, runm[:, :], d_g, waits=[(s_v, E.val("amax_x"))])
            dma(flatw, wamax[:, :], d_g,
                waits=[(s_v, E.val("wshmax_7"))])
            E.mark("flat", d_g)
            dma(t_ccin.ap(), amax2[:, :], d_g, waits=[(s_v, E.val("amax2"))])
            E.mark("ccin_w", d_g)
            if not dry:
                g.wait_ge(d_g, E.val("ccin_w"))
                g.collective_compute("AllReduce", ALU.max,
                                     replica_groups=[list(range(NCORE))],
                                     ins=[t_ccin.ap()], outs=[t_ccout.ap()],
                                     ).then_inc(s_cc, 1)
            E.bump(s_cc, 1)
            dma(ccsb[:, :], t_ccout.ap(), d_g, waits=[(s_cc, 1)])
            E.mark("cc_read", d_g)
            for ot in range(NOT):
                dma(wqtap[ot * 128:(ot + 1) * 128, :], wqt[:, :], d_qw,
                    waits=[(s_sc, E.val(f"wtreS_{ot}_30")),
                           (s_v, E.val(f"wtreV_{ot}_31"))])
            for ot in range(NOT):
                for rc in range(NRC):
                    j = 4 * ot + rc
                    dma(bass.AP(t_out, rc * 512 * C + ot * 128,
                                [[C, 128], [128 * C, 4], [1, 128]]),
                        bass.AP(outp[j % 2], 0, [[512, 128], [128, 4], [1, 128]]),
                        d_out[j % 2], waits=[(s_v, E.val(f"add_{j}"))])
            if not dry:
                g.wait_ge(d_out[0], W16 * 64)
                g.wait_ge(d_out[1], W16 * 64)

        def emit_scalar(s):
            def op(meth, *a, waits=(), mark=None, **kw):
                pre = E.cnt.get(id(s_sc), 0)
                if not dry:
                    if pre > 0:
                        s.wait_ge(s_sc, pre)
                    for sm, vv in waits:
                        if vv > 0:
                            s.wait_ge(sm, vv)
                    getattr(s, meth)(*a, **kw).then_inc(s_sc, 1)
                E.bump(s_sc, 1)
                if mark:
                    E.mark(mark, s_sc)

            def dma(out, in_, dsem, waits=()):
                if not dry:
                    for sm, vv in waits:
                        if vv > 0:
                            s.wait_ge(sm, vv)
                    s.dma_start(out=out, in_=in_).then_inc(dsem, W16)
                E.bump(dsem, W16)

            op("activation", waits=[(d_prm, W16 * 19)], out=scl[0:1, 0:1],
               in_=p_ls[0:1, 0:1], func=AF.Exp, mark="exp")

            def stats(rt):
                op("activation", waits=[(d_x[rt % 2], W16 * (rt // 2 + 1))],
                   out=junkb,
                   in_=xst[rt % 2][:, :], func=AF.Copy, accum_out=stat[:, 0:1])
                op("activation", out=junkb, in_=xst[rt % 2][:, :],
                   func=AF.Square, accum_out=stat[:, 1:2], mark=f"acc_{rt}")
                op("activation", waits=[(s_v, E.val(f"var_{rt}"))],
                   out=stat[:, 8:9], in_=stat[:, 7:8], func=AF.Sqrt,
                   mark=f"sqrt_{rt}")

            def trevac_tile(rt):
                rc = rt // 4
                for kk in range(32):
                    gi = rt * 32 + kk
                    op("copy", waits=[(s_t, E.val(f"tr_{gi}"))],
                       out=xnt[rc % 2][:, kk * 512 + (rt % 4) * 128:
                                       kk * 512 + (rt % 4) * 128 + 128],
                       in_=ptbank(gi), mark=f"tre_{gi}")

            for rt in range(3):
                stats(rt)
            for p in range(128):
                w_ = [(d_g, E.val("kpad_done"))] if p == 0 else ()
                dma(kstack[p:p + 1, :],
                    bass.AP(t_kpad, 4224 - p, [[1, 1], [1, C]]), d_ks, waits=w_)
            for ot in range(NOT):
                dma(bcol[:, ot:ot + 1],
                    t_bias.ap()[ot * 128:(ot + 1) * 128, 0:1], d_bc)
            for n in range(3, 16):
                trevac_tile(n - 3)
                stats(n)
            for n in range(13, 16):
                trevac_tile(n)
            for rt in range(NRT):
                for pair in (2 * rt, 2 * rt + 1):
                    for k in range(0, 32, 2):
                        w_ = [(s_t, E.val(f"wtr_{pair}_{k}"))]
                        if k == 0 and pair > 0:
                            w_.append((d_qw, W16 * pair))
                        op("copy", waits=w_,
                           out=wqt[:, k * 128:(k + 1) * 128],
                           in_=pt8bank(E_tridx[f"wtr_{pair}_{k}"]),
                           mark=f"wtreS_{pair}_{k}")
                rc_, rl_ = rt // 4, rt % 4
                for k in range(0, 32, 2):
                    op("copy", waits=[(s_t, E.val(f"qtr_{rt}_{k}"))],
                       out=bass.AP(arena, k * 2048 + rc_ * 512 + rl_ * 128,
                                   [[65536, 128], [1, 128]]).bitcast(FP8),
                       in_=pt8bank(E_tridx[f"qtr_{rt}_{k}"]),
                       mark=f"qeS_{rt}_{k}")
            E.mark("xe_last_S", s_sc)
            for ot in range(NOT):
                for rc in range(NRC):
                    j = 4 * ot + rc
                    w_ = [(s_t, E.val(f"ffn_{ot}_{rc}"))]
                    if j == 0:
                        w_.append((s_v, E.val("s4b")))
                        w_.append((d_bc, W16 * 32))
                    if j >= 2:
                        w_.append((s_v, E.val(f"sgm_{j - 2}")))
                    op("activation", waits=w_, out=gst[j % 2][:, :],
                       in_=pf[rc][:, :], func=AF.Silu, scale=s4b[:, 0:1],
                       bias=bcol[:, ot:ot + 1], mark=f"silu_{ot}_{rc}")

        def emit_vector(v):
            def op(meth, *a, waits=(), mark=None, **kw):
                pre = E.cnt.get(id(s_v), 0)
                if not dry:
                    if pre > 0:
                        v.wait_ge(s_v, pre)
                    for sm, vv in waits:
                        if vv > 0:
                            v.wait_ge(sm, vv)
                    getattr(v, meth)(*a, **kw).then_inc(s_v, 1)
                E.bump(s_v, 1)
                if mark:
                    E.mark(mark, s_v)

            op("tensor_scalar", waits=[(s_sc, E.val("exp")), (d_prm, W16 * 19)],
               out=scl[0:1, 1:2], in0=scl[0:1, 0:1], scalar1=0.5, scalar2=None,
               op0=ALU.mult, mark="h")
            for i in range(8):
                op("reduce_max", waits=[(d_wsh[i % 2], W16 * (i // 2 + 1))],
                   out=stat[:, 12:13], in_=wld[i % 2][:, :], axis=AX.X,
                   apply_absolute_value=True)
                if i == 0:
                    op("tensor_copy", out=wamax[:, :], in_=stat[:, 12:13],
                       mark="wshmax_0")
                else:
                    op("tensor_max", out=wamax[:, :], in0=wamax[:, :],
                       in1=stat[:, 12:13], mark=f"wshmax_{i}")
            op("tensor_copy", waits=[(s_t, E.val("Bmm"))], out=bvec[:, 0:1],
               in_=pf[0][0:64, 0:1])
            op("tensor_copy", out=bvec[:, 1:2], in_=pf[1][0:64, 0:1])
            op("tensor_copy", waits=[(s_t, E.val("bc0"))], out=dcol[:, :],
               in_=pf[2][0:128, 0:1])
            op("tensor_copy", out=hcol[:, :], in_=pf[3][0:128, 0:1])
            pairs = [(p_ct, bvec[:, 0:1], bvec[:, 1:2]),
                     (p_ct, p_pp[:, 0:1], p_pp[:, 1:2]),
                     (p_qq, bvec[:, 0:1], bvec[:, 1:2]),
                     (p_qq, p_pp[:, 0:1], p_pp[:, 1:2])]
            for qi, (cc_, bre, bim) in enumerate(pairs):
                op("tensor_mul", out=ww[:, 0:1], in0=cc_[:, 0:1], in1=bre)
                op("tensor_mul", out=ww[:, 1:2], in0=cc_[:, 1:2], in1=bim)
                op("tensor_add", out=vre[:, qi:qi + 1], in0=ww[:, 0:1],
                   in1=ww[:, 1:2])
                op("tensor_mul", out=ww[:, 0:1], in0=cc_[:, 0:1], in1=bim)
                op("tensor_mul", out=ww[:, 1:2], in0=cc_[:, 1:2], in1=bre)
                op("tensor_sub", out=vim[:, qi:qi + 1], in0=ww[:, 0:1],
                   in1=ww[:, 1:2])
            op("tensor_scalar", out=vmim[:, :], in0=vim[:, :], scalar1=-1.0,
               scalar2=None, op0=ALU.mult, mark="vv")
            op("tensor_mul", out=kv[:, 0:1], in0=p_lam[:, 0:1],
               in1=hcol[0:64, 0:1])
            op("tensor_scalar", out=kv[:, 1:2], in0=kv[:, 0:1], scalar1=-1.0,
               scalar2=None, op0=ALU.mult)
            op("tensor_mul", out=kv[:, 2:3], in0=kv[:, 1:2], in1=kv[:, 1:2])
            op("tensor_mul", out=kv[:, 3:4], in0=p_lam[:, 1:2],
               in1=hcol[0:64, 0:1], mark="kvv")
            for c in range(4):
                op("tensor_scalar", waits=[(d_tb, W16 * (c + 1))],
                   out=dimF[:, :], in0=tbch[c % 2][:, :], scalar1=kv[:, 3:4],
                   scalar2=None, op0=ALU.subtract)
                op("scalar_tensor_tensor", out=magF[:, :], in0=dimF[:, :],
                   scalar=1.0, in1=dimF[:, :], op0=ALU.mult, op1=ALU.mult)
                op("tensor_scalar", out=magF[:, :], in0=magF[:, :],
                   scalar1=kv[:, 2:3], scalar2=None, op0=ALU.add)
                op("reciprocal", out=magF[:, :], in_=magF[:, :])
                op("tensor_scalar", out=freF[:, :], in0=magF[:, :],
                   scalar1=kv[:, 1:2], scalar2=None, op0=ALU.mult)
                op("scalar_tensor_tensor", out=fimF[:, :], in0=dimF[:, :],
                   scalar=-1.0, in1=magF[:, :], op0=ALU.mult, op1=ALU.mult,
                   mark=f"F_{c}")
                kkw = [(s_t, E.val(f"kk_{c}"))]
                if c >= 1:
                    kkw.append((d_g, E.val(f"krr_{c - 1}")))
                op("tensor_scalar", waits=kkw,
                   out=kkre[:, 0:512], in0=pf[0][0:4, :],
                   scalar1=hcol[0:4, 0:1], scalar2=None, op0=ALU.mult)
                op("tensor_scalar", out=kkre[:, 512:1024], in0=pf[2][0:4, :],
                   scalar1=hcol[0:4, 0:1], scalar2=None, op0=ALU.mult)
                op("tensor_scalar", out=kkim[:, 0:512], in0=pf[1][0:4, :],
                   scalar1=hcol[0:4, 0:1], scalar2=None, op0=ALU.mult)
                op("tensor_scalar", out=kkim[:, 512:1024], in0=pf[3][0:4, :],
                   scalar1=hcol[0:4, 0:1], scalar2=None, op0=ALU.mult,
                   mark=f"kkevac_{c}")
            op("tensor_scalar", waits=[(d_g, E.val("kr_done"))],
               out=tmpa[:, :], in0=krre[3][:, :], scalar1=1.0, scalar2=None,
               op0=ALU.add)
            op("tensor_mul", out=tmpb[:, :], in0=tmpa[:, :], in1=tmpa[:, :])
            op("tensor_mul", out=narim[:, :], in0=krim[3][:, :],
               in1=krim[3][:, :])
            op("tensor_add", out=tmpb[:, :], in0=tmpb[:, :], in1=narim[:, :])
            op("reciprocal", out=tmpb[:, :], in_=tmpb[:, :])
            op("tensor_mul", out=narim[:, :], in0=krre[1][:, :],
               in1=krre[2][:, :])
            op("tensor_mul", out=arim[:, :], in0=krim[1][:, :],
               in1=krim[2][:, :])
            op("tensor_sub", out=narim[:, :], in0=narim[:, :],
               in1=arim[:, :])
            op("tensor_mul", out=arim[:, :], in0=krre[1][:, :],
               in1=krim[2][:, :])
            op("tensor_mul", out=arre[:, :], in0=krim[1][:, :],
               in1=krre[2][:, :])
            op("tensor_add", out=arim[:, :], in0=arim[:, :],
               in1=arre[:, :])
            op("tensor_mul", out=arre[:, :], in0=narim[:, :], in1=tmpa[:, :])
            op("tensor_mul", out=krre[1][:, :], in0=arim[:, :],
               in1=krim[3][:, :])
            op("tensor_add", out=arre[:, :], in0=arre[:, :], in1=krre[1][:, :])
            op("tensor_mul", out=arre[:, :], in0=arre[:, :], in1=tmpb[:, :])
            op("tensor_mul", out=krre[1][:, :], in0=arim[:, :], in1=tmpa[:, :])
            op("tensor_mul", out=narim[:, :], in0=narim[:, :],
               in1=krim[3][:, :])
            op("tensor_sub", out=krre[1][:, :], in0=krre[1][:, :],
               in1=narim[:, :])
            op("tensor_mul", out=krre[1][:, :], in0=krre[1][:, :],
               in1=tmpb[:, :])
            op("tensor_sub", out=krre[0][:, :], in0=krre[0][:, :],
               in1=arre[:, :])
            op("tensor_sub", out=krim[0][:, :], in0=krim[0][:, :],
               in1=krre[1][:, :])
            op("tensor_mul", out=arre[:, :], in0=p_tg[:, :], in1=krim[0][:, :])
            op("tensor_sub", out=arre[:, :], in0=krre[0][:, :], in1=arre[:, :])
            op("tensor_mul", out=arim[:, :], in0=p_tg[:, :], in1=krre[0][:, :])
            op("tensor_add", out=arim[:, :], in0=krim[0][:, :], in1=arim[:, :])
            op("tensor_scalar", out=narim[:, :], in0=arim[:, :], scalar1=-1.0,
               scalar2=None, op0=ALU.mult, mark="ar")
            op("tensor_copy", waits=[(s_t, E.val("st1"))], out=As[:, 0:64],
               in_=pf[0][0:64, 0:64])
            op("tensor_copy", out=As[:, 64:128], in_=pf[1][0:64, 0:64])
            op("tensor_mul", out=Ats[:, 0:64], in0=As[:, 0:64],
               in1=p_w2re[:, :])
            op("tensor_mul", out=tmpa[:, :], in0=As[:, 64:128],
               in1=p_w2im[:, :])
            op("tensor_sub", out=Ats[:, 0:64], in0=Ats[:, 0:64],
               in1=tmpa[:, :])
            op("tensor_mul", out=Ats[:, 64:128], in0=As[:, 0:64],
               in1=p_w2im[:, :])
            op("tensor_mul", out=tmpa[:, :], in0=As[:, 64:128],
               in1=p_w2re[:, :])
            op("tensor_add", out=Ats[:, 64:128], in0=Ats[:, 64:128],
               in1=tmpa[:, :], mark="tw")
            op("tensor_copy", waits=[(s_t, E.val("st2"))], out=kt_sb[:, :],
               in_=pf[2][0:64, 0:64], mark="ktevac")

            inv = 1.0 / 4096.0

            def lnstats(rt):
                w_ = [(s_sc, E.val(f"acc_{rt}")),
                      (d_x[rt % 2], W16 * (rt // 2 + 1))]
                op("reduce_max", waits=w_, out=stat[:, 2:3],
                   in_=xst[rt % 2][:, :], axis=AX.X)
                op("tensor_reduce", out=stat[:, 3:4], in_=xst[rt % 2][:, :],
                   axis=AX.X, op=ALU.min)
                op("tensor_scalar", out=statm[:, rt:rt + 1], in0=stat[:, 0:1],
                   scalar1=inv, scalar2=None, op0=ALU.mult)
                op("tensor_scalar", out=stat[:, 5:6], in0=stat[:, 1:2],
                   scalar1=inv, scalar2=None, op0=ALU.mult)
                op("scalar_tensor_tensor", out=stat[:, 6:7],
                   in0=statm[:, rt:rt + 1], scalar=-1.0,
                   in1=statm[:, rt:rt + 1], op0=ALU.mult, op1=ALU.mult)
                op("tensor_add", out=stat[:, 7:8], in0=stat[:, 5:6],
                   in1=stat[:, 6:7])
                op("tensor_scalar", out=stat[:, 7:8], in0=stat[:, 7:8],
                   scalar1=1e-5, scalar2=None, op0=ALU.add, mark=f"var_{rt}")
                op("reciprocal", waits=[(s_sc, E.val(f"sqrt_{rt}"))],
                   out=statr[:, rt:rt + 1], in_=stat[:, 8:9])
                op("tensor_sub", out=stat[:, 10:11], in0=stat[:, 2:3],
                   in1=statm[:, rt:rt + 1])
                op("tensor_sub", out=stat[:, 11:12], in0=statm[:, rt:rt + 1],
                   in1=stat[:, 3:4])
                op("tensor_max", out=stat[:, 10:11], in0=stat[:, 10:11],
                   in1=stat[:, 11:12])
                op("tensor_mul", out=stat[:, 10:11], in0=stat[:, 10:11],
                   in1=statr[:, rt:rt + 1])
                if rt == 0:
                    op("tensor_copy", out=runm[:, :], in_=stat[:, 10:11])
                else:
                    op("tensor_max", out=runm[:, :], in0=runm[:, :],
                       in1=stat[:, 10:11])

            def lnnorm(rt):
                op("tensor_scalar",
                   waits=([(s_t, E.val(f"tr_{rt * 32 - 1}"))] if rt >= 1
                          else ()),
                   out=xnb[:, :], in0=xst[rt % 2][:, :],
                   scalar1=statm[:, rt:rt + 1], scalar2=statr[:, rt:rt + 1],
                   op0=ALU.subtract, op1=ALU.mult, mark=f"xn_{rt}")

            def convevac(rc):
                for tt in range(NTT):
                    i = 32 * rc + tt
                    w_ = [(s_t, E.val(f"conv_{rc}_{tt}"))]
                    if i >= 4:
                        w_.append((d_sw[i % 4], W16 * (i // 4)))
                    op("scalar_tensor_tensor", waits=w_, out=stv[i % 4][:, :],
                       in0=xnt[rc % 2][:, tt * 512:(tt + 1) * 512],
                       scalar=dcol[:, 0:1], in1=pf[tt % 2][:, :],
                       op0=ALU.mult, op1=ALU.add, mark=f"sevac_{rc}_{tt}")

            lnstats(0)
            lnstats(1)
            lnnorm(0)
            for n in range(2, 16):
                lnstats(n)
                lnnorm(n - 1)
                if n - 1 == 8:
                    convevac(0)
                if n - 1 == 12:
                    convevac(1)
            lnnorm(15)
            E.mark("amax_x", s_v)
            convevac(2)
            convevac(3)
            op("tensor_reduce", waits=[(d_g, E.val("flat"))],
               out=amax2[0:1, 0:1], in_=flatx, axis=AX.X, op=ALU.max)
            op("tensor_reduce", out=amax2[0:1, 1:2], in_=flatw,
               axis=AX.X, op=ALU.max, mark="amax2")
            op("tensor_scalar", waits=[(d_g, E.val("cc_read"))],
               out=scl[0:1, 2:3], in0=ccsb[0:1, 0:1], scalar1=1e-12,
               scalar2=float(1.0 / 448.0), op0=ALU.max, op1=ALU.mult)
            op("tensor_scalar", out=scl[0:1, 3:4], in0=ccsb[0:1, 1:2],
               scalar1=1e-12, scalar2=float(1.0 / 448.0), op0=ALU.max,
               op1=ALU.mult)
            op("tensor_scalar", out=scl[0:1, 4:5], in0=scl[0:1, 2:3],
               scalar1=2.0, scalar2=None, op0=ALU.mult)
            op("reciprocal", out=scl[0:1, 5:6], in_=scl[0:1, 4:5])
            op("tensor_scalar", out=scl[0:1, 6:7], in0=scl[0:1, 3:4],
               scalar1=2.0, scalar2=None, op0=ALU.mult)
            op("reciprocal", out=scl[0:1, 7:8], in_=scl[0:1, 6:7])
            op("tensor_mul", out=scl[0:1, 8:9], in0=scl[0:1, 2:3],
               in1=scl[0:1, 3:4])
            op("tensor_scalar", out=scl[0:1, 9:10], in0=scl[0:1, 8:9],
               scalar1=4.0, scalar2=None, op0=ALU.mult, mark="scales")
            op("tensor_copy", waits=[(s_t, E.val("bcastmm"))], out=i2sxb[:, :],
               in_=pf[2][0:128, 0:1])
            op("tensor_copy", out=i2swb[:, :], in_=pf[3][0:128, 0:1])
            op("tensor_copy", out=s4b[:, :], in_=pf[2][0:128, 1:2], mark="s4b")
            for rt in range(NRT):
                op("tensor_mul", out=rs2[:, rt:rt + 1],
                   in0=statr[:, rt:rt + 1], in1=i2sxb[:, 0:1])
            for ot in range(NOT):
                for h in range(2):
                    i = 2 * ot + h
                    op("tensor_scalar",
                       waits=[(d_w[i % 2], W16 * (i // 2 + 1))],
                       out=wqb[:, h * 2048:h * 2048 + 2048],
                       in0=wld[i % 2][:, :], scalar1=i2swb[:, 0:1],
                       scalar2=None, op0=ALU.mult, mark=f"wsc_{i}")
                if ot % 2 == 0:
                    rt = ot // 2
                    w_ = [(d_x[rt % 2], W16 * (8 + rt // 2 + 1))]
                    if rt == 0:
                        w_.append((s_sc, E.val("acc_15")))
                    if rt >= 2:
                        w_.append((s_t, E.val(f"qtr_{rt - 2}_31")))
                    op("tensor_scalar", waits=w_, out=q8r[rt % 2][:, :],
                       in0=xst[rt % 2][:, :], scalar1=statm[:, rt:rt + 1],
                       scalar2=rs2[:, rt:rt + 1], op0=ALU.subtract,
                       op1=ALU.mult, mark=f"q8r_{rt}")
                for k in range(1, 32, 2):
                    w_ = [(s_t, E.val(f"wtr_{ot}_{k}"))]
                    if k == 1 and ot > 0:
                        w_.append((d_qw, W16 * ot))
                    op("tensor_copy", waits=w_,
                       out=wqt[:, k * 128:(k + 1) * 128],
                       in_=pt8bank(E_tridx[f"wtr_{ot}_{k}"]),
                       mark=f"wtreV_{ot}_{k}")
                if ot % 2 == 1:
                    rt = ot // 2
                    rc_, rl_ = rt // 4, rt % 4
                    for k in range(1, 32, 2):
                        op("tensor_copy",
                           waits=[(s_t, E.val(f"qtr_{rt}_{k}"))],
                           out=bass.AP(arena,
                                       k * 2048 + rc_ * 512 + rl_ * 128,
                                       [[65536, 128], [1, 128]]).bitcast(FP8),
                           in_=pt8bank(E_tridx[f"qtr_{rt}_{k}"]),
                           mark=f"qe_{rt}_{k}")
            E.mark("xe_last_V", s_v)
            for ot in range(NOT):
                for rc in range(NRC):
                    j = 4 * ot + rc
                    w_ = [(s_sc, E.val(f"silu_{ot}_{rc}")),
                          (d_sb[j % 2], W16 * (j // 2 + 1))]
                    op("tensor_mul", waits=w_, out=sgst[j % 2][:, :],
                       in0=gst[j % 2][:, :], in1=stld[j % 2][:, :],
                       mark=f"sgm_{j}")
                    for k in range(4):
                        w2 = [(s_t, E.val(f"sgtr_{j}_{k}"))]
                        if k == 0:
                            w2.append((d_xc[j % 2], W16 * (j // 2 + 1)))
                            if j >= 2:
                                w2.append((d_out[j % 2], W16 * (j // 2)))
                        op("tensor_add", waits=w2,
                           out=outp[j % 2][:, k * 128:(k + 1) * 128],
                           in0=xcb[j % 2][:, k * 128:(k + 1) * 128],
                           in1=ptbank(j * 4 + k))
                    E.mark(f"add_{j}", s_v)

        def emit_tensor(t):
            def mm(out, lhsT, rhs, waits=(), mark=None, inc=True, **kw):
                if not dry:
                    for sm, vv in waits:
                        if vv > 0:
                            t.wait_ge(sm, vv)
                    ins = t.matmul(out, lhsT, rhs, **kw)
                    if inc:
                        ins.then_inc(s_t, 1)
                if inc:
                    E.bump(s_t, 1)
                if mark:
                    E.mark(mark, s_t)

            mm(pf[0][0:64, 0:1], p_vre[:, :], p_bp[:, :],
               waits=[(d_prm, W16 * 19)], start=True, stop=True)
            mm(pf[1][0:64, 0:1], p_vim[:, :], p_bp[:, :], start=True,
               stop=True, mark="Bmm")
            mm(pf[2][0:128, 0:1], p_ones[:, :], p_dd[0:1, 0:1], start=True,
               stop=True)
            mm(pf[3][0:128, 0:1], p_ones[:, :], scl[0:1, 1:2],
               waits=[(s_v, E.val("h"))], start=True, stop=True, mark="bc0")
            for c in range(4):
                w0 = [(s_v, E.val(f"F_{c}")), (s_v, E.val("vv"))]
                if c >= 1:
                    w0.append((s_v, E.val(f"kkevac_{c - 1}")))
                for hf in range(2):
                    sl = slice(hf * 512, hf * 512 + 512)
                    pre, pim = (pf[0], pf[1]) if hf == 0 else (pf[2], pf[3])
                    mm(pre[0:4, :], vre[:, :], freF[:, sl],
                       waits=(w0 if hf == 0 else ()), start=True, stop=False)
                    mm(pre[0:4, :], vmim[:, :], fimF[:, sl], start=False,
                       stop=True)
                    mm(pim[0:4, :], vim[:, :], freF[:, sl], start=True,
                       stop=False)
                    mm(pim[0:4, :], vre[:, :], fimF[:, sl], start=False,
                       stop=True, mark=(f"kk_{c}" if hf == 1 else None))
            mm(pf[0][0:64, 0:64], arre[:, :], p_e1re[:, :],
               waits=[(s_v, E.val("ar"))], start=True, stop=False)
            mm(pf[0][0:64, 0:64], narim[:, :], p_e1im[:, :], start=False,
               stop=True)
            mm(pf[1][0:64, 0:64], arim[:, :], p_e1re[:, :], start=True,
               stop=False)
            mm(pf[1][0:64, 0:64], arre[:, :], p_e1im[:, :], start=False,
               stop=True, mark="st1")
            mm(pf[2][0:64, 0:64], p_e2re[:, :], Ats[:, 0:64],
               waits=[(s_v, E.val("tw"))], start=True, stop=False)
            mm(pf[2][0:64, 0:64], p_e2imn[:, :], Ats[:, 64:128], start=False,
               stop=True, mark="st2")

            def xtr(rc):
                for k in range(128):
                    rt = 4 * rc + k // 32
                    kk = k % 32
                    gi = rc * 128 + k
                    w_ = []
                    if kk == 0:
                        w_.append((s_v, E.val(f"xn_{rt}")))
                    if gi >= 4:
                        w_.append((s_sc, E.val(f"tre_{gi - 4}")))
                    mm(ptbank(gi), xnb[:, kk * 128:(kk + 1) * 128],
                       p_idb[:, :], waits=w_, is_transpose=True, start=True,
                       stop=True, mark=f"tr_{gi}")

            def conv(rc):
                for tt in range(NTT):
                    for ttau in range(tt + 1):
                        d = 128 * (tt - ttau)
                        w_ = []
                        if ttau == 0:
                            if tt == 0:
                                w_.append((s_sc, E.val(f"tre_{rc * 128 + 127}")))
                                if rc == 0:
                                    w_.append((d_ks, W16 * 128))
                            prev = (f"sevac_{rc}_{tt - 2}" if tt >= 2 else
                                    f"sevac_{rc - 1}_{tt + 30}")
                            w_.append((s_v, E.val(prev)))
                        mm(pf[tt % 2][:, :], kstack[:, d:d + 128],
                           xnt[rc % 2][:, ttau * 512:(ttau + 1) * 512],
                           waits=w_, start=(ttau == 0), stop=(ttau == tt),
                           inc=(ttau == tt),
                           mark=(f"conv_{rc}_{tt}" if ttau == tt else None))

            xtr(0)
            xtr(1)
            conv(0)
            xtr(2)
            conv(1)
            xtr(3)
            conv(2)
            conv(3)
            mm(pf[2][0:128, 0:1], p_ones[:, :], scl[0:1, 5:6],
               waits=[(s_v, E.val("scales"))], start=True, stop=True)
            mm(pf[3][0:128, 0:1], p_ones[:, :], scl[0:1, 7:8], start=True,
               stop=True)
            mm(pf[2][0:128, 1:2], p_ones[:, :], scl[0:1, 9:10], start=True,
               stop=True, mark="bcastmm")

            def phw_tr(lab, in_ap, extra):
                g = E_tridx[lab]
                w_ = list(extra)
                psem, plab = ring[g - 4]
                w_.append((s_sc if psem == "s_sc" else s_v,
                           E.val(evac_of(plab))))
                mm(pt8bank(g), in_ap, p_id8[:, :], waits=w_,
                   is_transpose=True, start=True, stop=True, mark=lab)

            for rt in range(NRT):
                for pair in (2 * rt, 2 * rt + 1):
                    for k in range(32):
                        ex = []
                        if k == 0:
                            ex.append((s_v, E.val(f"wsc_{2 * pair}")))
                        if k == 16:
                            ex.append((s_v, E.val(f"wsc_{2 * pair + 1}")))
                        phw_tr(f"wtr_{pair}_{k}",
                               wqb[:, k * 128:(k + 1) * 128], ex)
                for k in range(32):
                    ex = []
                    if k == 0:
                        ex.append((s_v, E.val(f"q8r_{rt}")))
                    phw_tr(f"qtr_{rt}_{k}",
                           q8r[rt % 2][:, k * 128:(k + 1) * 128], ex)

            def sgtr(ot):
                for rc in range(NRC):
                    j = 4 * ot + rc
                    for k in range(4):
                        w_ = []
                        if k == 0:
                            w_.append((s_v, E.val(f"sgm_{j}")))
                            if j == 0:
                                w_.append((s_v, E.val("xe_last_V")))
                                w_.append((s_sc, E.val("xe_last_S")))
                            elif j >= 1:
                                w_.append((s_v, E.val(f"add_{j - 1}")))
                        mm(ptbank(j * 4 + k),
                           sgst[j % 2][:, k * 128:(k + 1) * 128], p_idb[:, :],
                           waits=w_, is_transpose=True, start=True, stop=True,
                           mark=f"sgtr_{j}_{k}")

            for ot in range(NOT):
                for half in range(2):
                    for jj in range(NPAIR):
                        for rc2 in range(2):
                            rc = 2 * half + rc2
                            w_ = []
                            if jj == 0 and rc2 == 0:
                                if ot == 0 and half == 0:
                                    w_.append((s_v, E.val("xe_last_V")))
                                    w_.append((s_sc, E.val("xe_last_S")))
                                w_.append((d_wq[ot % 2], W16 * (ot // 2 + 1)))
                                if ot >= 1:
                                    w_.append((s_sc, E.val(f"silu_{ot - 1}_{rc}")))
                                    w_.append((s_sc,
                                               E.val(f"silu_{ot - 1}_{rc + 1}")))
                            lhsT = wqldb[ot % 2][:, jj * 256:(jj + 1) * 256] \
                                .rearrange("p (two m) -> p two m", two=2)
                            rhs = bass.AP(arena, 2 * jj * 2048 + rc * 512,
                                          [[65536, 128], [2048, 2], [1, 512]]
                                          ).bitcast(FP8)
                            mm(pf[rc][:, :], lhsT, rhs, waits=w_,
                               start=(jj == 0), stop=(jj == NPAIR - 1),
                               inc=(jj == NPAIR - 1), perf_mode=DRMODE,
                               mark=(f"ffn_{ot}_{rc}" if jj == NPAIR - 1
                                     else None))
                    if half == 0 and ot >= 1:
                        sgtr(ot - 1)
            sgtr(NOT - 1)

        if not dry:
            @block.sync
            def _(g):
                emit_sync(g)

            @block.gpsimd
            def _(g):
                emit_gpsimd(g)

            @block.scalar
            def _(s):
                emit_scalar(s)

            @block.vector
            def _(v):
                emit_vector(v)

            @block.tensor
            def _(t):
                emit_tensor(t)
        else:
            emit_sync(None)
            emit_gpsimd(None)
            emit_scalar(None)
            emit_vector(None)
            emit_tensor(None)

    return nc


def _host_constants():
    k = np.arange(4096, dtype=np.float64)
    tb = np.tan(np.pi * k / 4096.0)
    tbb = np.broadcast_to(tb, (64, 4096)).astype(np.float32).copy()
    tg = tb.reshape(64, 64).astype(np.float32)
    k1 = np.arange(64.0)
    m0 = np.arange(64.0)
    e1 = np.exp(2j * np.pi * np.outer(k1, m0) / 64.0)
    w2 = np.exp(2j * np.pi * np.outer(k1, m0) / 4096.0)
    e2 = np.exp(2j * np.pi * np.outer(k1, m0) / 64.0) / 4096.0
    return dict(
        tb=tbb, tg=tg,
        e1re=e1.real.astype(np.float32), e1im=e1.imag.astype(np.float32),
        w2re=w2.real.astype(np.float32), w2im=w2.imag.astype(np.float32),
        e2re=e2.real.astype(np.float32), e2imn=(-e2.imag).astype(np.float32),
        idb=np.eye(128).astype(ml_dtypes.bfloat16),
        id8=np.eye(128).astype(ml_dtypes.float8_e4m3),
        onesf=np.ones((1, 128), np.float32),
    )


def _build_program():
    marks = {}
    nc0 = bass.Bass("TRN2", target_bir_lowering=False)
    build_nc(marks, dry=True, nc=nc0)
    nc = bass.Bass("TRN2", target_bir_lowering=False)
    build_nc(marks, dry=False, nc=nc)
    return nc


def kernel(L0_attn_u, L0_attn_Lambda_ri, L0_attn_p_ri, L0_attn_q_ri,
           L0_attn_Vc_ri, L0_attn_Ct_ri, L0_attn_Bp, L0_attn_D,
           L0_attn_log_step, L0_ffn_W, L0_ffn_b):
    x = np.ascontiguousarray(np.asarray(L0_attn_u, np.float32)).reshape(16384, C)
    W = np.ascontiguousarray(np.asarray(L0_ffn_W, np.float32))
    b = np.ascontiguousarray(np.asarray(L0_ffn_b, np.float32)).reshape(C, 1)
    lam = np.ascontiguousarray(np.asarray(L0_attn_Lambda_ri, np.float32))
    pp = np.ascontiguousarray(np.asarray(L0_attn_p_ri, np.float32))
    qq = np.ascontiguousarray(np.asarray(L0_attn_q_ri, np.float32))
    ct = np.ascontiguousarray(np.asarray(L0_attn_Ct_ri, np.float32)).reshape(64, 2)
    bp = np.ascontiguousarray(np.asarray(L0_attn_Bp, np.float32)).reshape(64, 1)
    dd = np.asarray(L0_attn_D, np.float32).reshape(1, 1)
    ls = np.asarray(L0_attn_log_step, np.float32).reshape(1, 1)
    vc = np.asarray(L0_attn_Vc_ri, np.float32)
    vctre = np.ascontiguousarray(vc[:, :, 0].T)
    vctim = np.ascontiguousarray(vc[:, :, 1].T)
    consts = _host_constants()

    nc = _build_program()

    in_maps = []
    for c_ in range(NCORE):
        in_maps.append(dict(
            x=np.ascontiguousarray(x[c_ * RPC:(c_ + 1) * RPC]),
            w=W, wsh=np.ascontiguousarray(W[c_ * 512:(c_ + 1) * 512]),
            bias=b, lam=lam, pp=pp, qq=qq, ct=ct, bp=bp, dd=dd, ls=ls,
            vctre=vctre, vctim=vctim, **consts))

    res = run_bass_kernel_spmd(nc, in_maps, core_ids=list(range(NCORE)))
    out = np.concatenate([np.asarray(r["out"], np.float32)
                          for r in res.results], axis=0)
    return out.reshape(4, 4096, C)


if __name__ == "__main__":
    import reference as ref
    inputs = {k: np.asarray(v) for k, v in ref.setup_inputs(0).items()}
    got = kernel(**inputs)
    exp = np.asarray(ref.reference(**inputs))
    d = got.astype(np.float64) - exp.astype(np.float64)
    print("ratio", float((d ** 2).mean() / (exp.astype(np.float64) ** 2).mean()))
